# version 01 (final)
# baseline (speedup 1.0000x reference)
import contextlib
import numpy as np
import ml_dtypes
import concourse.bass as bass
import concourse.mybir as mybir
from concourse.bass_utils import run_bass_kernel_spmd

F32 = mybir.dt.float32
BF16 = mybir.dt.bfloat16
AF = mybir.ActivationFunctionType
ALU = mybir.AluOpType
AX = mybir.AxisListType

B, N, D = 4, 4096, 1024
T = 2048
HALO = 257
CW = HALO + T
NCH = 8
HEADS, DH = 16, 64
LAYERS = 6
DI = 2730
NM1 = 22
LASTW = DI - 21 * 128
QT = 512
HF = 1024
EPS_LN = 1e-5
EPS_LA = 1e-6
NEG = -1.0e30

DEBUG_STAGE = None
RACE_MODE = False
BOOT_SKIP = set()
SHUF_MASK = [i ^ 1 for i in range(32)]


class Sem:
    def __init__(self, h, name):
        self.h, self.name, self.n = h, name, 0


class DmaSem:

    def __init__(self, subs):
        self.subs = list(subs)
        self.R = len(self.subs)
        self.idx = 0

    @property
    def n(self):
        return tuple(sub.n for sub in self.subs)


class Prog:
    ENGINES = ("tensor", "vector", "scalar", "gpsimd", "sync")
    ORDERED = ("tensor", "vector", "scalar")

    def __init__(self, fake=None):
        self.ops = {e: [] for e in self.ENGINES}
        self.waited = {e: {} for e in self.ENGINES}
        self.fake = fake

    def op(self, eng, fn, waits=(), inc=None, inc_by=1):
        flat = []
        for s, v in waits:
            if s is None:
                continue
            if isinstance(s, DmaSem):
                assert isinstance(v, tuple)
                flat.extend(zip(s.subs, v))
            else:
                flat.append((s, v))
        throttle = None
        if isinstance(inc, DmaSem):
            sub = inc.subs[inc.idx % inc.R]
            inc.idx += 1
            flat.append((sub, sub.n))
            sub.n += inc_by
            inc = None
            throttle = sub
        rw = []
        for s, v in flat:
            if v <= 0:
                continue
            if v > self.waited[eng].get(s.name, 0):
                self.waited[eng][s.name] = v
                rw.append((s, v))
        if inc is not None:
            inc.n += inc_by
        if throttle is not None:
            def fn2(h, fn=fn, sub=throttle):
                ins = fn(h)
                ins.then_inc(sub.h, 16)
                return None
            fn = fn2
        fk = None
        if self.fake is not None and eng in self.ORDERED:
            fk = (self.fake[eng], self.fake[eng].n)
            self.fake[eng].n += 1

        def run(h, fn=fn, rw=tuple(rw), inc=inc, inc_by=inc_by, fk=fk):
            if fk is not None and fk[1] > 0:
                h.wait_ge(fk[0].h, fk[1])
            for s, v in rw:
                h.wait_ge(s.h, v)
            ins = fn(h)
            if inc is not None:
                ins.then_inc(inc.h, inc_by)
            if fk is not None and ins is not None:
                ins.then_inc(fk[0].h, 1)
        self.ops[eng].append(run)
        return inc.n if inc is not None else None

    def wait(self, eng, waits):
        self.op(eng, lambda h: None, waits=waits)


class PsumSlots:

    def __init__(self):
        self.free = [(None, 0)] * 8

    def wait_of(self, slots):
        return [self.free[s] for s in slots]

    def consumed(self, slots, sem, count):
        for s in slots:
            self.free[s] = (sem, count)


_PERM64 = np.array([(i // 2) + 32 * (i % 2) for i in range(64)])


def _rotary_tables(p0):
    t = np.arange(p0 - 256, p0 + T, dtype=np.float64)
    t = np.maximum(t, 0.0)
    half = np.arange(0, DH, 2, dtype=np.float64)
    inv_freq = 1.0 / 10000.0 ** (half / DH)
    fr = (t[:, None] * inv_freq[None, :]).T
    base = (half + 0.4 * DH) / (1.4 * DH)
    power = (t - N // 2) / 128.0
    scale = (base[None, :] ** power[:, None]).T
    c, s = np.cos(fr), np.sin(fr)
    inv = 1.0 / scale
    TT = t.shape[0]
    A = np.zeros((4, DH, TT))
    for i in range(32):
        A[0, 2 * i] = 8.0 * c[i] * scale[i];     A[1, 2 * i] = -8.0 * s[i] * scale[i]
        A[0, 2 * i + 1] = 8.0 * c[i] * scale[i]; A[1, 2 * i + 1] = 8.0 * s[i] * scale[i]
        A[2, 2 * i] = c[i] * inv[i];             A[3, 2 * i] = -s[i] * inv[i]
        A[2, 2 * i + 1] = c[i] * inv[i];         A[3, 2 * i + 1] = s[i] * inv[i]
    return np.concatenate([A, A], axis=1).astype(np.float32)


def _masks_host(h):
    pi = np.arange(128)[:, None]
    r = np.arange(128)[None, :]
    s0 = np.where(r >= pi, 0.0, NEG).astype(np.float32)
    s2 = np.where(r <= pi, 0.0, NEG).astype(np.float32)
    zero = np.zeros((128, 128), np.float32)
    full = np.full((128, 128), NEG, np.float32)
    if h == 0:
        return np.stack([s0, s2, full, full, full])
    return np.stack([s0, s2, s0, zero, s0])


def _permute_qk_cols(w):
    idx = np.arange(3 * D)
    for hh in range(HEADS):
        for part in range(2):
            base = part * D + hh * 64
            idx[base:base + 64] = base + _PERM64
    return np.ascontiguousarray(w[:, :, idx])


def build_program():
    nc = bass.Bass("TRN2", target_bir_lowering=False,
                   detect_race_conditions=RACE_MODE)
    dt = nc.dram_tensor
    xT = dt("xT", [D, T], F32, kind="ExternalInput")
    wqkvoT = dt("wqkvoT", [D, 3 * D], F32, kind="ExternalInput")
    woutT = dt("woutT", [D, D], F32, kind="ExternalInput")
    boutI = dt("boutI", [128, NCH], F32, kind="ExternalInput")
    wqkvI = dt("wqkvI", [LAYERS, D, 3 * D], F32, kind="ExternalInput")
    woI = dt("woI", [LAYERS, D, D], F32, kind="ExternalInput")
    w1I = dt("w1I", [LAYERS, D, 2 * DI], F32, kind="ExternalInput")
    w2I = dt("w2I", [LAYERS, DI, D], F32, kind="ExternalInput")
    qsI = dt("qsI", [128, LAYERS], F32, kind="ExternalInput")
    ksI = dt("ksI", [128, LAYERS], F32, kind="ExternalInput")
    lngI = dt("lngI", [128, NCH * LAYERS], F32, kind="ExternalInput")
    lnbI = dt("lnbI", [128, NCH * LAYERS], F32, kind="ExternalInput")
    b1aI = dt("b1aI", [128, NM1 * LAYERS], F32, kind="ExternalInput")
    b1gI = dt("b1gI", [128, NM1 * LAYERS], F32, kind="ExternalInput")
    b2I = dt("b2I", [128, NCH * LAYERS], F32, kind="ExternalInput")
    tabsI = dt("tabsI", [4, 128, 256 + T], F32, kind="ExternalInput")
    masksI = dt("masksI", [5, 128, 128], F32, kind="ExternalInput")
    ones2I = dt("ones2I", [128, NCH * 16], F32, kind="ExternalInput")
    eyeI = dt("eyeI", [128, 128], F32, kind="ExternalInput")
    hvI = dt("hvI", [128, 1], F32, kind="ExternalInput")
    outE = dt("outE", [D, T], F32, kind="ExternalOutput")
    x1sp = dt("x1sp", [D, T], F32)
    kvin = dt("kvin", [D, D + 1], F32)
    kvout = dt("kvout", [D, D + 1], F32)
    agin = dt("agin", [D, HALO], F32)
    agout = dt("agout", [2 * D, HALO], F32)
    rowF = dt("rowF", [4, T], F32)
    rowH = dt("rowH", [16, 768], BF16)

    al = nc.alloc_sbuf_tensor
    stream = al("stream", [128, NCH * CW], F32)
    maskb = al("maskb", [128, 5 * 128], BF16)
    eyeb = al("eyeb", [128, 128], BF16)
    onesb = al("onesb", [128, 1], BF16)
    ones2b = al("ones2b", [128, NCH * 16], BF16)
    qsb = al("qsb", [128, LAYERS], F32)
    ksb = al("ksb", [128, LAYERS], F32)
    lngb = al("lngb", [128, NCH * LAYERS], F32)
    lnbb = al("lnbb", [128, NCH * LAYERS], F32)
    b1ab = al("b1ab", [128, NM1 * LAYERS], F32)
    b1gb = al("b1gb", [128, NM1 * LAYERS], F32)
    b2b = al("b2b", [128, NCH * LAYERS], F32)
    boutb = al("boutb", [128, NCH], F32)
    hvb = al("hvb", [128, 1], F32)

    nmx = al("nmx", [128, 16], F32)
    den = al("den", [128, 16], F32)
    rden = al("rden", [128, 16], F32)
    vpad = al("vpad", [128, 4], F32)
    dcol = al("dcol", [128, NCH], F32)
    AR_COLS = 67584
    arena = al("arena", [128, AR_COLS], BF16)

    rows = arena[:, 0:4096].bitcast(F32)
    bc1 = arena[:, 4096:6144].bitcast(F32)
    bc2 = arena[:, 6144:8192].bitcast(F32)
    rnrow = arena[0:16, 8192:8960]
    ARENA_PREFIX = 8960

    PS1 = nc.alloc_psum_tensor("PS1", [128, 2048], F32)
    PS2 = nc.alloc_psum_tensor("PS2", [128, 2048], F32)
    PT = PS2.ap().bitcast(BF16)

    def PS(slot, width=512):
        t_ = PS1 if slot < 4 else PS2
        c0 = (slot % 4) * 512
        return t_[:, c0:c0 + width]

    stk = contextlib.ExitStack()
    SS = {}
    for name in ("te", "ve", "se", "cc"):
        SS[name] = Sem(stk.enter_context(nc.semaphore(name)), name)
    s_te, s_ve, s_se, s_cc = (SS[k] for k in ("te", "ve", "se", "cc"))

    def _dma_pool(name, R):
        return DmaSem([Sem(stk.enter_context(nc.semaphore(f"{name}{r}")),
                           f"{name}{r}") for r in range(R)])
    s_wl = _dma_pool("wl", 10)
    s_sy = _dma_pool("sy", 6)
    s_st = _dma_pool("st", 4)

    fake = None
    if RACE_MODE:
        fake = {e: Sem(stk.enter_context(nc.semaphore("fk_" + e)), "fk_" + e)
                for e in Prog.ORDERED}
    p = Prog(fake)
    ps = PsumSlots()

    class Arena:
        def __init__(self):
            self.pos = 0

        def take(self, cols_bf16, dtype=BF16):
            cols_bf16 = (cols_bf16 + 63) & ~63
            st = self.pos
            self.pos += cols_bf16
            assert self.pos <= AR_COLS, f"arena overflow {self.pos}"
            ap = arena[:, st:st + cols_bf16]
            return ap.bitcast(F32) if dtype == F32 else ap

        def reset(self):
            self.pos = ARENA_PREFIX

    A = Arena()

    def stream_ap(c, t0, t1, shift=0):
        base = c * CW + HALO
        return stream[:, base + t0 - shift: base + t1 - shift]

    def xs_ap(c, t0, t1):
        return stream_ap(c, t0, t1, shift=1 if c >= 4 else 0)

    def stream3(t0, t1):
        return stream.ap().rearrange("p (c w) -> p c w", c=NCH)[
            :, :, HALO + t0:HALO + t1]

    def barrier():
        w = [(s_te, s_te.n), (s_ve, s_ve.n), (s_se, s_se.n),
             (s_wl, s_wl.n), (s_sy, s_sy.n), (s_cc, s_cc.n)]
        for e in Prog.ENGINES:
            p.wait(e, w)
        ps.free = [(None, 0)] * 8

    if "masks" not in BOOT_SKIP:
        p.op("gpsimd", lambda h: h.dma_start(
            out=maskb[:, :].rearrange("p (m r) -> p m r", m=5),
            in_=masksI.ap().rearrange("m p r -> p m r")),
            inc=s_wl, inc_by=16)
    if "eye" not in BOOT_SKIP:
        p.op("gpsimd", lambda h: h.dma_start(out=eyeb[:, :], in_=eyeI[:, :]),
             inc=s_wl, inc_by=16)
    if "ones2" not in BOOT_SKIP:
        p.op("gpsimd", lambda h: h.dma_start(
            out=ones2b[:, :], in_=ones2I[:, :]), inc=s_wl, inc_by=16)
    if "vecs" not in BOOT_SKIP:
        for dst, src in ((qsb, qsI), (ksb, ksI), (lngb, lngI), (lnbb, lnbI),
                         (b1ab, b1aI), (b1gb, b1gI), (b2b, b2I),
                         (boutb, boutI), (hvb, hvI)):
            p.op("sync", (lambda h, d=dst, s=src: h.dma_start(
                out=d[:, :], in_=s[:, :])), inc=s_sy, inc_by=16)
    for c in range(NCH):
        if "stream" not in BOOT_SKIP:
            p.op("sync", (lambda h, c=c: h.dma_start(
                out=stream_ap(c, 0, T), in_=xT[c * 128:(c + 1) * 128, :])),
                inc=s_sy, inc_by=16)
        if "memset" not in BOOT_SKIP:
            p.op("vector", (lambda h, c=c: h.memset(
                stream[:, c * CW:c * CW + HALO], 0.0)), inc=s_ve)
    if "memset" not in BOOT_SKIP:
        p.op("vector", lambda h: h.memset(onesb[:, :], 1.0), inc=s_ve)
    barrier()

    def ln_to_bcast(t0, shifted, xsl, x2sl, ready=(), span=HF):
        te_prev = gf["xsl_te"][0]
        nbs = span // 512
        for c in range(NCH):
            src = xs_ap(c, t0, t0 + span) if shifted else stream_ap(c, t0, t0 + span)
            p.op("vector", (lambda h, src=src: h.tensor_copy(xsl[:, 0:span], src)),
                 waits=[(s_te, te_prev)] +
                       (ps.wait_of([4, 5]) + list(ready) if c == 0 else []),
                 inc=s_ve)
            vn = s_ve.n
            p.op("scalar", (lambda h, src=src: h.activation(
                x2sl[:, 0:span], src, AF.Square)),
                waits=[(s_te, te_prev)] +
                      (ps.wait_of([6, 7]) + list(ready) if c == 0 else []),
                 inc=s_se)
            sn = s_se.n
            for nb in range(nbs):
                p.op("tensor", (lambda h, c=c, nb=nb: h.matmul(
                    PS2[0:1, nb * 512:(nb + 1) * 512], onesb[:, :],
                    xsl[:, nb * 512:(nb + 1) * 512],
                    start=(c == 0), stop=(c == NCH - 1))),
                    waits=[(s_ve, vn)] + (ps.wait_of([4, 5]) if c == 0 else []),
                    inc=s_te)
                p.op("tensor", (lambda h, c=c, nb=nb: h.matmul(
                    PS2[0:1, 1024 + nb * 512:1024 + (nb + 1) * 512], onesb[:, :],
                    x2sl[:, nb * 512:(nb + 1) * 512],
                    start=(c == 0), stop=(c == NCH - 1))),
                    waits=[(s_se, sn)] + (ps.wait_of([6, 7]) if c == 0 else []),
                    inc=s_te)
            te_prev = s_te.n
        te_done = s_te.n
        gf["xsl_te"][0] = te_done
        R = rows[0:1, 0:span]
        R2 = rows[0:1, 1024:1024 + span]
        p.op("vector", (lambda h: h.tensor_scalar(
            out=R, in0=PS2[0:1, 0:span], scalar1=1.0 / D, scalar2=None,
            op0=ALU.mult)), waits=[(s_te, te_done), (s_sy, gf["rows_sy"]),
                                   (s_se, gf["rows_se"])],
            inc=s_ve)
        p.op("vector", (lambda h: h.tensor_scalar(
            out=R2, in0=PS2[0:1, 1024:1024 + span], scalar1=1.0 / D,
            scalar2=None, op0=ALU.mult)), inc=s_ve)
        p.op("vector", (lambda h: h.tensor_tensor(
            PS2[0:1, 0:span], R, R, ALU.mult)), inc=s_ve)
        p.op("vector", (lambda h: h.tensor_tensor(
            R2, R2, PS2[0:1, 0:span], ALU.subtract)), inc=s_ve)
        ps.consumed([4, 5, 6, 7], s_ve, s_ve.n)
        p.op("vector", (lambda h: h.tensor_scalar(
            out=R2, in0=R2, scalar1=EPS_LN, scalar2=None,
            op0=ALU.add)), inc=s_ve)
        p.op("scalar", (lambda h: h.activation(R2, R2, AF.Ln)),
             waits=[(s_ve, s_ve.n)], inc=s_se)
        p.op("scalar", (lambda h: h.activation(R2, R2, AF.Exp, scale=-0.5)),
             inc=s_se)
        gf["rows_se"] = s_se.n
        p.op("vector", (lambda h: h.tensor_copy(R2, R2)),
             waits=[(s_se, s_se.n)], inc=s_ve)
        p.op("vector", (lambda h: h.scalar_tensor_tensor(
            out=R, in0=R, scalar=-1.0, in1=R2,
            op0=ALU.mult, op1=ALU.mult)), inc=s_ve)
        ve_rows = s_ve.n
        p.op("sync", (lambda h: h.dma_start(out=rowF[0:1, 0:span], in_=R2)),
             waits=[(s_ve, ve_rows), (s_wl, gf["rowF_wl"])], inc=s_sy, inc_by=16)
        p.op("sync", (lambda h: h.dma_start(out=rowF[1:2, 0:span], in_=R)),
             inc=s_sy, inc_by=16)
        sy0 = s_sy.n
        gf["rows_sy"] = sy0
        p.op("gpsimd", (lambda h: h.dma_start(
            out=bc1[:, 0:span],
            in_=rowF.ap()[0:1, 0:span].partition_broadcast(128))),
            waits=[(s_sy, sy0), (s_ve, gf["bc_ve"])], inc=s_wl, inc_by=16)
        p.op("gpsimd", (lambda h: h.dma_start(
            out=bc2[:, 0:span],
            in_=rowF.ap()[1:2, 0:span].partition_broadcast(128))),
            inc=s_wl, inc_by=16)
        gf["rowF_wl"] = s_wl.n
        return s_wl.n

    xsl_free = [0]
    gf = {"rows_sy": s_sy.n, "rowF_wl": s_wl.n, "rowH_wl": s_wl.n, "bc_ve": 0,
          "rows_se": 0, "rnrow_sy": s_sy.n, "xsl_te": [0, 0]}

    def dump_arena_and_stop(region):
        w = [(s_ve, s_ve.n), (s_te, s_te.n), (s_se, s_se.n), (s_wl, s_wl.n),
             (s_sy, s_sy.n), (s_cc, s_cc.n)]
        f32 = region.dtype == F32
        cols = region.shape[1]
        nj = min(8, (cols + 2047) // 2048)
        for j in range(nj):
            c0 = j * 2048
            c1 = min(cols, c0 + 2048)
            if f32:
                p.op("sync", (lambda h, j=j, c0=c0, c1=c1: h.dma_start(
                    out=outE[j * 128:(j + 1) * 128, 0:c1 - c0],
                    in_=region[:, c0:c1])), waits=w, inc=s_st, inc_by=16)
            else:
                p.op("gpsimd", (lambda h, j=j, c0=c0, c1=c1: h.dma_start(
                    out=outE[j * 128:(j + 1) * 128, 0:c1 - c0],
                    in_=region[:, c0:c1])), waits=w, inc=s_st, inc_by=16)
        p.wait("sync", [(s_st, s_st.n)])
        p.wait("gpsimd", [(s_st, s_st.n)])

    def dump_stream_and_stop():
        w = [(s_ve, s_ve.n), (s_te, s_te.n), (s_se, s_se.n), (s_wl, s_wl.n),
             (s_sy, s_sy.n), (s_cc, s_cc.n)]
        for c in range(NCH):
            p.op("sync", (lambda h, c=c: h.dma_start(
                out=outE[c * 128:(c + 1) * 128, :], in_=stream_ap(c, 0, T))),
                waits=w, inc=s_st, inc_by=16)
        p.wait("sync", [(s_st, s_st.n)])

    if DEBUG_STAGE == "x0":
        dump_stream_and_stop()
        return nc, p

    A.reset()
    h0b = A.take(NCH * HF)
    qbuf = A.take(NCH * T)
    ktm = A.take(4 * D)
    vtm8 = A.take(4 * D)
    wslab = [A.take(NCH * 128) for _ in range(4)]
    wmov = A.take(NCH * 512)
    kvslot = A.take(2 * 1040, F32)
    xsl = A.take(HF)
    x2sl = A.take(HF)
    t32 = A.take(2 * HF, F32)
    kvb = A.take(NCH * D)
    ksumb = A.take(64)
    azslot = A.take(2 * 512)
    zbb = A.take(2 * 1024, F32)
    slab_free = [(None, 0)] * 4
    wmov_free = [(None, 0)]
    kv_wl_prev = [s_wl.n] * NCH
    kv_done = [s_wl.n] * NCH
    h0b_free = [0]
    ktm_free = [0]
    kvslot_free = [s_wl.n, s_wl.n]

    def ln_apply_h0(t0, wl_bc, dst, extra=()):
        for c in range(NCH):
            p.op("vector", (lambda h, c=c: h.tensor_tensor(
                t32[:, 0:HF], stream_ap(c, t0, t0 + HF), bc1[:, 0:HF], ALU.mult)),
                waits=[(s_wl, wl_bc)] + list(extra), inc=s_ve)
            p.op("vector", (lambda h, c=c: h.tensor_tensor(
                dst(c), t32[:, 0:HF], bc2[:, 0:HF], ALU.add)), inc=s_ve)
        gf["bc_ve"] = s_ve.n
        return s_ve.n

    for hf in range(2):
        t0 = hf * HF
        wl_bc = ln_to_bcast(t0, False, xsl, x2sl, ready=[(s_ve, s_ve.n)])
        h0b_guard = [(s_te, h0b_free[0])]
        ve_h0 = ln_apply_h0(t0, wl_bc, lambda c: h0b[:, c * HF:(c + 1) * HF],
                            extra=h0b_guard)

        for m in range(NCH):
            slot = wslab[m % 4]
            p.op("gpsimd", (lambda h, m=m, slot=slot: h.dma_start(
                out=slot.rearrange("p (c n) -> p c n", c=NCH),
                in_=wqkvoT.ap()[:, m * 128:(m + 1) * 128]
                    .rearrange("(c p) n -> p c n", p=128))),
                waits=[slab_free[m % 4]], inc=s_wl, inc_by=16)
            wl_m = s_wl.n
            sp = [0, 1] if m % 2 == 0 else [2, 3]
            for kc in range(NCH):
                p.op("tensor", (lambda h, kc=kc, slot=slot: h.ldweights(
                    slot[:, kc * 128:(kc + 1) * 128])),
                    waits=[(s_wl, wl_m), (s_ve, ve_h0)] +
                          (ps.wait_of(sp) if kc == 0 else []), inc=s_te)
                for nb in range(2):
                    p.op("tensor", (lambda h, kc=kc, nb=nb, sp=sp, slot=slot: h.matmul(
                        PS(sp[nb]), slot[:, kc * 128:(kc + 1) * 128],
                        h0b[:, kc * HF + nb * 512: kc * HF + (nb + 1) * 512],
                        start=(kc == 0), stop=(kc == NCH - 1))), inc=s_te)
            slab_free[m % 4] = (s_te, s_te.n)
            te_m = s_te.n
            p.op("vector", (lambda h, m=m, sp=sp, t0=t0: h.tensor_scalar(
                out=qbuf[:, m * T + t0: m * T + t0 + HF],
                in0=PS1[:, sp[0] * 512: sp[0] * 512 + HF],
                scalar1=0.0, scalar2=EPS_LA, op0=ALU.max, op1=ALU.add)),
                waits=[(s_te, te_m)], inc=s_ve)
            ps.consumed(sp, s_ve, s_ve.n)

        for grp in range(2):
            tg = grp * 4
            for part, dst, relu in ((0, ktm, True), (1, vtm8, False)):
                for cg in range(2):
                    p.op("gpsimd", (lambda h, part=part, cg=cg: h.dma_start(
                        out=wmov.rearrange("p (c n) -> p c n", c=NCH),
                        in_=wqkvoT.ap()[:, D + part * D + cg * 512:
                                        D + part * D + (cg + 1) * 512]
                            .rearrange("(c p) n -> p c n", p=128))),
                        waits=[wmov_free[0]], inc=s_wl, inc_by=16)
                    wl_cg = s_wl.n
                    for tt in range(4):
                        sp = [tt % 4]
                        for kc in range(NCH):
                            p.op("tensor", (lambda h, kc=kc, tt=tt, tg=tg: h.ldweights(
                                h0b[:, kc * HF + (tg + tt) * 128:
                                    kc * HF + (tg + tt + 1) * 128])),
                                waits=[(s_wl, wl_cg)] +
                                      (ps.wait_of(sp) if kc == 0 else []), inc=s_te)
                            p.op("tensor", (lambda h, kc=kc, sp=sp, tt=tt, tg=tg: h.matmul(
                                PS(sp[0]),
                                h0b[:, kc * HF + (tg + tt) * 128:
                                    kc * HF + (tg + tt + 1) * 128],
                                wmov[:, kc * 512:(kc + 1) * 512],
                                start=(kc == 0), stop=(kc == NCH - 1))), inc=s_te)
                        te_t = s_te.n
                        if relu:
                            p.op("vector", (lambda h, tt=tt, sp=sp, dst=dst, cg=cg: h.tensor_scalar(
                                out=dst[:, tt * D + cg * 512: tt * D + cg * 512 + 512],
                                in0=PS(sp[0]), scalar1=0.0, scalar2=EPS_LA,
                                op0=ALU.max, op1=ALU.add)),
                                waits=[(s_te, te_t), (s_te, ktm_free[0])], inc=s_ve)
                        else:
                            p.op("vector", (lambda h, tt=tt, sp=sp, dst=dst, cg=cg: h.tensor_copy(
                                dst[:, tt * D + cg * 512: tt * D + cg * 512 + 512],
                                PS(sp[0]))),
                                waits=[(s_te, te_t), (s_te, ktm_free[0])], inc=s_ve)
                        ps.consumed(sp, s_ve, s_ve.n)
                    wmov_free[0] = (s_te, s_te.n)
            ve_kv = s_ve.n
            for dc in range(NCH):
                sp = [4, 5] if dc % 2 == 0 else [6, 7]
                kss = dc % 2
                for tt in range(4):
                    p.op("tensor", (lambda h, dc=dc, tt=tt: h.ldweights(
                        ktm[:, tt * D + dc * 128: tt * D + (dc + 1) * 128])),
                        waits=[(s_ve, ve_kv)] +
                              (ps.wait_of(sp + [kss]) if tt == 0 else []), inc=s_te)
                    for nb in range(2):
                        p.op("tensor", (lambda h, dc=dc, tt=tt, nb=nb, sp=sp: h.matmul(
                            PS(sp[nb]),
                            ktm[:, tt * D + dc * 128: tt * D + (dc + 1) * 128],
                            vtm8[:, tt * D + nb * 512: tt * D + (nb + 1) * 512],
                            start=(tt == 0), stop=(tt == 3))), inc=s_te)
                    p.op("tensor", (lambda h, dc=dc, tt=tt, kss=kss: h.matmul(
                        PS1[:, kss * 512: kss * 512 + 1],
                        ktm[:, tt * D + dc * 128: tt * D + (dc + 1) * 128],
                        onesb[:, :], start=(tt == 0), stop=(tt == 3))), inc=s_te)
                te_dc = s_te.n
                ko = 0
                p.op("vector", (lambda h, sp=sp, ko=ko: h.tensor_copy(
                    kvslot[:, ko:ko + 512], PS(sp[0]))),
                    waits=[(s_te, te_dc), (s_wl, kvslot_free[0])], inc=s_ve)
                p.op("vector", (lambda h, sp=sp, ko=ko: h.tensor_copy(
                    kvslot[:, ko + 512:ko + 1024], PS(sp[1]))), inc=s_ve)
                p.op("vector", (lambda h, ko=ko, kss=kss: h.tensor_copy(
                    kvslot[:, ko + 1024:ko + 1025],
                    PS1[:, kss * 512: kss * 512 + 1])), inc=s_ve)
                ve_dc = s_ve.n
                ps.consumed(sp + [kss], s_ve, ve_dc)
                first_dma = (hf == 0 and grp == 0)
                p.op("gpsimd", (lambda h, dc=dc, ko=ko, first_dma=first_dma: h.dma_start(
                    out=kvin[dc * 128:(dc + 1) * 128, :],
                    in_=kvslot[:, ko:ko + 1025],
                    accum_op=(ALU.bypass if first_dma else ALU.add))),
                    waits=[(s_ve, ve_dc), (s_wl, kv_wl_prev[dc])],
                    inc=s_wl, inc_by=16)
                kv_wl_prev[dc] = s_wl.n
                kv_done[dc] = s_wl.n
                kvslot_free[0] = s_wl.n
            ktm_free[0] = s_te.n

    p.op("gpsimd", (lambda h: h.collective_compute(
        "AllReduce", ALU.add, replica_groups=[[0, 1], [2, 3], [4, 5], [6, 7]],
        ins=[kvin.ap().opt()], outs=[kvout.ap().opt()])),
        waits=[(s_wl, s_wl.n)], inc=s_cc)
    p.op("gpsimd", (lambda h: h.dma_start(
        out=kvb.rearrange("p (c n) -> p c n", c=NCH),
        in_=kvout.ap()[:, 0:D].rearrange("(c p) n -> p c n", p=128))),
        waits=[(s_cc, s_cc.n)], inc=s_wl, inc_by=16)
    p.op("gpsimd", (lambda h: h.dma_start(
        out=ksumb[:, 0:NCH].rearrange("p (c o) -> p c o", o=1),
        in_=kvout.ap()[:, D:D + 1].rearrange("(c p) n -> p c n", p=128),
        allow_slow_non_contiguous=True)),
        inc=s_wl, inc_by=16)
    wl_kv = s_wl.n

    for dc in range(NCH):
        p.op("tensor", (lambda h, dc=dc: h.ldweights(ksumb[:, dc:dc + 1])),
             waits=[(s_wl, wl_kv)] + (ps.wait_of([4, 5, 6, 7]) if dc == 0 else []),
             inc=s_te)
        for nb in range(4):
            p.op("tensor", (lambda h, dc=dc, nb=nb: h.matmul(
                PS2[0:1, nb * 512:(nb + 1) * 512], ksumb[:, dc:dc + 1],
                qbuf[:, dc * T + nb * 512: dc * T + (nb + 1) * 512],
                start=(dc == 0), stop=(dc == NCH - 1))), inc=s_te)
    te_z = s_te.n
    Rz0 = rows[0:1, 0:HF]
    Rz1 = rows[0:1, 1024:1024 + HF]
    p.op("vector", (lambda h: h.tensor_scalar(
        out=Rz0, in0=PS2[0:1, 0:HF], scalar1=EPS_LA, scalar2=None,
        op0=ALU.add)), waits=[(s_te, te_z), (s_sy, gf["rows_sy"]),
                              (s_se, gf["rows_se"])], inc=s_ve)
    p.op("vector", (lambda h: h.tensor_scalar(
        out=Rz1, in0=PS2[0:1, HF:2 * HF], scalar1=EPS_LA,
        scalar2=None, op0=ALU.add)), inc=s_ve)
    p.op("vector", (lambda h: h.reciprocal(Rz0, Rz0)), inc=s_ve)
    p.op("vector", (lambda h: h.reciprocal(Rz1, Rz1)), inc=s_ve)
    ps.consumed([4, 5, 6, 7], s_ve, s_ve.n)
    p.op("sync", (lambda h: h.dma_start(out=rowF[2:3, 0:HF], in_=Rz0)),
         waits=[(s_ve, s_ve.n), (s_wl, gf["rowF_wl"])], inc=s_sy, inc_by=16)
    p.op("sync", (lambda h: h.dma_start(out=rowF[3:4, 0:HF], in_=Rz1)),
         inc=s_sy, inc_by=16)
    sy_z = s_sy.n
    gf["rows_sy"] = sy_z

    az_used = [0, 0]
    az_ve = [0, 0]
    zb_used = [0, 0]
    for tc in range(4):
        tcc = tc * 512
        p.op("gpsimd", (lambda h, tc=tc: h.dma_start(
            out=zbb[:, (tc % 2) * 512:(tc % 2) * 512 + 512],
            in_=rowF.ap()[2 + tc // 2:3 + tc // 2, (tc % 2) * 512:(tc % 2) * 512 + 512]
                .partition_broadcast(128))),
            waits=[(s_sy, sy_z), (s_ve, zb_used[tc % 2])], inc=s_wl, inc_by=16)
        wl_z = s_wl.n
        for fcg in range(2):
            for ec in range(NCH):
                spA = [ec % 2]
                if True:
                    for dc in range(NCH):
                        p.op("tensor", (lambda h, ec=ec, dc=dc: h.ldweights(
                            kvb[:, dc * D + ec * 128: dc * D + (ec + 1) * 128])),
                            waits=[(s_wl, wl_kv)] +
                                  (ps.wait_of(spA) if dc == 0 else []), inc=s_te)
                        p.op("tensor", (lambda h, ec=ec, dc=dc, spA=spA, tcc=tcc: h.matmul(
                            PS(spA[0]),
                            kvb[:, dc * D + ec * 128: dc * D + (ec + 1) * 128],
                            qbuf[:, dc * T + tcc: dc * T + tcc + 512],
                            start=(dc == 0), stop=(dc == NCH - 1))), inc=s_te)
                    te_ec = s_te.n
                    p.op("vector", (lambda h, ec=ec, spA=spA, tc=tc: h.tensor_tensor(
                        azslot[:, (ec % 2) * 512:(ec % 2) * 512 + 512],
                        PS(spA[0]), zbb[:, (tc % 2) * 512:(tc % 2) * 512 + 512],
                        ALU.mult)),
                        waits=[(s_te, te_ec), (s_wl, wl_z),
                               (s_te, az_used[ec % 2])], inc=s_ve)
                    ps.consumed(spA, s_ve, s_ve.n)
                    az_ve[ec % 2] = s_ve.n
                    zb_used[tc % 2] = s_ve.n
                slot = wslab[ec % 4]
                p.op("gpsimd", (lambda h, ec=ec, fcg=fcg, slot=slot: h.dma_start(
                    out=slot[:, 0:512].rearrange("p (c n) -> p c n", c=4),
                    in_=woutT.ap()[ec * 128:(ec + 1) * 128,
                                   fcg * 512:(fcg + 1) * 512]
                        .rearrange("p (c n) -> p c n", c=4))),
                    waits=[slab_free[ec % 4]], inc=s_wl, inc_by=16)
                wl_wo = s_wl.n
                for fi in range(4):
                    p.op("tensor", (lambda h, fi=fi, slot=slot, ec=ec: h.ldweights(
                        slot[:, fi * 128:(fi + 1) * 128])),
                        waits=[(s_wl, wl_wo), (s_ve, az_ve[ec % 2])] +
                              (ps.wait_of([4 + fi]) if ec == 0 else []),
                        inc=s_te)
                    p.op("tensor", (lambda h, fi=fi, ec=ec, slot=slot: h.matmul(
                        PS(4 + fi), slot[:, fi * 128:(fi + 1) * 128],
                        azslot[:, (ec % 2) * 512:(ec % 2) * 512 + 512],
                        start=(ec == 0), stop=(ec == NCH - 1))), inc=s_te)
                slab_free[ec % 4] = (s_te, s_te.n)
                az_used[ec % 2] = s_te.n
            te_fcg = s_te.n
            for fi in range(4):
                fc = fcg * 4 + fi
                p.op("vector", (lambda h, fi=fi, fc=fc, tcc=tcc: h.scalar_tensor_tensor(
                    out=stream_ap(fc, tcc, tcc + 512), in0=PS(4 + fi),
                    scalar=boutb[:, fc:fc + 1],
                    in1=stream_ap(fc, tcc, tcc + 512),
                    op0=ALU.add, op1=ALU.add)),
                    waits=[(s_te, te_fcg)], inc=s_ve)
                ps.consumed([4 + fi], s_ve, s_ve.n)
    ve_x1 = s_ve.n

    if DEBUG_STAGE == "x1":
        dump_stream_and_stop()
        return nc, p

    for c in range(NCH):
        p.op("sync", (lambda h, c=c: h.dma_start(
            out=x1sp[c * 128:(c + 1) * 128, :], in_=stream_ap(c, 0, T))),
            waits=[(s_ve, ve_x1)], inc=s_sy, inc_by=16)
    sy_x1 = s_sy.n
    p.wait("vector", [(s_sy, sy_x1)])

    for hf in range(2):
        t0 = hf * HF
        wl_bc = ln_to_bcast(t0, False, xsl, x2sl, ready=[(s_ve, ve_x1), (s_sy, sy_x1)])
        ln_apply_h0(t0, wl_bc, lambda c, t0=t0: stream_ap(c, t0, t0 + HF))
    ve_y0 = s_ve.n

    def halo_ag():
        ready = [(s_ve, s_ve.n), (s_te, s_te.n)]
        p.op("sync", (lambda h: h.dma_start(
            out=agin.ap().rearrange("(c p) j -> p c j", p=128),
            in_=stream.ap().rearrange("p (c w) -> p c w", c=NCH)[:, :, T:CW])),
            waits=ready + [(s_cc, s_cc.n)], inc=s_sy, inc_by=16)
        p.op("gpsimd", (lambda h: h.collective_compute(
            "AllGather", ALU.bypass,
            replica_groups=[[0, 1], [2, 3], [4, 5], [6, 7]],
            ins=[agin.ap().opt()], outs=[agout.ap().opt()])),
            waits=[(s_sy, s_sy.n)], inc=s_cc)
        p.op("sync", (lambda h: h.dma_start(
            out=stream.ap().rearrange("p (c w) -> p c w", c=NCH)[:, :, 0:HALO],
            in_=agout.ap()[0:D, :].rearrange("(c p) j -> p c j", p=128))),
            waits=[(s_cc, s_cc.n), (s_ve, s_ve.n), (s_se, s_se.n)],
            inc=s_sy, inc_by=16)
        p.op("vector", (lambda h: h.tensor_tensor(
            stream.ap().rearrange("p (c w) -> p c w", c=NCH)[:, :, 0:HALO],
            stream.ap().rearrange("p (c w) -> p c w", c=NCH)[:, :, 0:HALO],
            hvb[:, 0:1].unsqueeze(1).broadcast_to([128, NCH, HALO]), ALU.mult)),
            waits=[(s_sy, s_sy.n)], inc=s_ve)
        return s_ve.n

    halo_ag()

    if DEBUG_STAGE == "h2":
        dump_stream_and_stop()
        return nc, p

    barrier()
    A.reset()
    xsb = A.take(NCH * 768)
    qtil = A.take(NCH * 768)
    ktil = A.take(NCH * 768)
    vtm = A.take(6 * D)
    q2s = A.take(768)
    us = A.take(768)
    ups = A.take(768)
    attn_sb = [A.take(4 * 384) for _ in range(2)]
    attnT_sb = [A.take(4 * 384) for _ in range(2)]
    aout = A.take(NCH * QT)
    wsl2 = [A.take(NCH * 128) for _ in range(4)]
    wvp = A.take(NCH * D)
    rnb = A.take(NCH * 768)
    tabb = A.take(4 * 768)
    attn_arena_end = A.pos

    A.reset()
    FQ = 512
    xnf = A.take(NCH * FQ)
    hbuf = A.take(NM1 * FQ)
    w2res = A.take(NM1 * 1024)
    w1sl = [A.take(NCH * 256) for _ in range(2)]
    lnscr = A.take(2 * FQ)
    ggs = lnscr[:, 0:FQ]
    xslF = lnscr[:, 0:FQ]
    x2slF = lnscr[:, FQ:2 * FQ]
    t32F = lnscr.bitcast(F32)
    ffn_arena_end = A.pos

    st8 = {
        "stream_ve": s_ve.n, "qk_se": 0,
        "xsb_te": 0, "qtil_te": 0, "ktil_te": 0, "vtm_te": 0, "aout_te": 0,
        "attn_free": [(None, 0), (None, 0)],
        "attnT_free": [(None, 0), (None, 0)],
        "nmx_free": [(None, 0)] * 4,
        "den_free": [(None, 0)] * 4,
        "tab_ve": 0, "rnb_ve": 0,
        "slab2": [(None, 0)] * 4,
        "wvp": (None, 0),
        "w1": [(None, 0)] * 2,
        "w2res": (None, 0),
    }

    def norm_rot(buf, w0, spn, sc_ap, tA, tB, ss_c0, wl_tab):
        sps = list(range(4 + ss_c0 // 512, 4 + (ss_c0 + spn + 511) // 512))
        ve_in = s_ve.n
        te_prev = q2s_free[0]
        for c in range(NCH):
            p.op("scalar", (lambda h, c=c: h.activation(
                q2s[:, 0:spn], buf[:, c * 768 + w0: c * 768 + w0 + spn],
                AF.Square)), waits=[(s_ve, ve_in), (s_te, te_prev)], inc=s_se)
            sn = s_se.n
            nb0 = 0
            while nb0 < spn:
                w_ = min(512 - (ss_c0 + nb0) % 512, spn - nb0)
                p.op("tensor", (lambda h, c=c, nb0=nb0, w_=w_: h.matmul(
                    PS2[0:16, ss_c0 + nb0: ss_c0 + nb0 + w_],
                    ones2b[:, c * 16:(c + 1) * 16], q2s[:, nb0:nb0 + w_],
                    start=(c == 0), stop=(c == NCH - 1))),
                    waits=[(s_se, sn)] +
                          (ps.wait_of(sps) if (c == 0 and nb0 == 0) else []),
                    inc=s_te)
                nb0 += w_
            te_prev = s_te.n
        te_ss = s_te.n
        q2s_free[0] = te_ss
        nr = rows[0:16, 0:spn]
        p.op("vector", (lambda h, nr=nr: h.tensor_scalar(
            out=nr, in0=PS2[0:16, ss_c0:ss_c0 + spn],
            scalar1=1e-24, scalar2=None, op0=ALU.max)),
            waits=[(s_te, te_ss), (s_sy, gf["rows_sy"]),
                   (s_se, gf["rows_se"])], inc=s_ve)
        ps.consumed(sps, s_ve, s_ve.n)
        p.op("scalar", (lambda h, nr=nr: h.activation(nr, nr, AF.Ln)),
            waits=[(s_ve, s_ve.n)], inc=s_se)
        p.op("scalar", (lambda h, nr=nr: h.activation(
            rnrow[:, 0:spn], nr, AF.Exp, scale=-0.5)),
            waits=[(s_sy, gf["rnrow_sy"])], inc=s_se)
        gf["rows_se"] = s_se.n
        p.op("sync", (lambda h: h.dma_start(out=rowH[:, 0:spn], in_=rnrow[:, 0:spn])),
             waits=[(s_se, s_se.n), (s_wl, gf["rowH_wl"])], inc=s_sy, inc_by=16)
        sy_r = s_sy.n
        gf["rnrow_sy"] = sy_r
        for c in range(NCH):
            p.op("gpsimd", (lambda h, c=c: h.dma_start(
                out=rnb[:, c * 768:c * 768 + spn][0:64, :],
                in_=rowH.ap()[2 * c:2 * c + 1, 0:spn].partition_broadcast(64))),
                waits=[(s_sy, sy_r), (s_ve, st8["rnb_ve"])], inc=s_wl, inc_by=16)
            p.op("gpsimd", (lambda h, c=c: h.dma_start(
                out=rnb[:, c * 768:c * 768 + spn][64:128, :],
                in_=rowH.ap()[2 * c + 1:2 * c + 2, 0:spn].partition_broadcast(64))),
                inc=s_wl, inc_by=16)
        wl_rn = s_wl.n
        gf["rowH_wl"] = wl_rn
        for c in range(NCH):
            bs = buf[:, c * 768 + w0: c * 768 + w0 + spn]
            p.op("vector", (lambda h, bs=bs, c=c: h.scalar_tensor_tensor(
                out=us[:, 0:spn], in0=bs, scalar=sc_ap,
                in1=rnb[:, c * 768:c * 768 + spn],
                op0=ALU.mult, op1=ALU.mult)),
                waits=[(s_wl, wl_rn)], inc=s_ve)
            p.op("vector", (lambda h: h.stream_shuffle(
                ups[:, 0:spn], us[:, 0:spn], SHUF_MASK)), inc=s_ve)
            p.op("vector", (lambda h: h.tensor_tensor(
                us[:, 0:spn], us[:, 0:spn], tA[:, w0:w0 + spn], ALU.mult)),
                waits=[(s_wl, wl_tab)], inc=s_ve)
            p.op("vector", (lambda h: h.tensor_tensor(
                ups[:, 0:spn], ups[:, 0:spn], tB[:, w0:w0 + spn], ALU.mult)),
                inc=s_ve)
            p.op("vector", (lambda h, bs=bs: h.tensor_tensor(
                bs, us[:, 0:spn], ups[:, 0:spn], ALU.add)), inc=s_ve)
        st8["rnb_ve"] = s_ve.n
        return s_ve.n

    q2s_free = [0]
    scr_free = [0]

    def attn_quarter(li, Q):
        Qs = Q * QT
        first = (Q == 0)
        xcol0 = 0 if first else 256
        span = 768 if first else 512
        p.op("gpsimd", (lambda h, Qs=Qs: h.dma_start(
            out=tabb.rearrange("p (g n) -> p g n", g=4),
            in_=tabsI.ap()[:, :, Qs:Qs + 768].rearrange("g p n -> p g n"))),
            waits=[(s_ve, st8["tab_ve"])], inc=s_wl, inc_by=16)
        wl_tab = s_wl.n
        for c in range(NCH):
            p.op("vector", (lambda h, c=c, Qs=Qs, xcol0=xcol0, span=span: h.tensor_copy(
                xsb[:, c * 768 + xcol0: c * 768 + xcol0 + span],
                xs_ap(c, Qs - 256 + xcol0, Qs + 512))),
                waits=[(s_ve, st8["stream_ve"]), (s_te, st8["xsb_te"])], inc=s_ve)
        if Q > 0:
            for c in range(NCH):
                p.op("vector", (lambda h, c=c, Qs=Qs: h.tensor_tensor(
                    stream_ap(c, Qs - 1, Qs), stream_ap(c, Qs - 1, Qs),
                    dcol[:, c:c + 1], ALU.add)), inc=s_ve)
        ve_xsb = s_ve.n

        for m in range(16):
            is_q = m < 8
            slot = wsl2[m % 4]
            p.op("gpsimd", (lambda h, m=m, li=li, slot=slot: h.dma_start(
                out=slot.rearrange("p (c n) -> p c n", c=NCH),
                in_=wqkvI.ap()[li, :, m * 128:(m + 1) * 128]
                    .rearrange("(c p) n -> p c n", p=128))),
                waits=[st8["slab2"][m % 4]], inc=s_wl, inc_by=16)
            wl_m = s_wl.n
            nn = 512 if is_q else span
            rc0 = 256 if is_q else xcol0
            sp = [0, 1] if m % 2 == 0 else [2, 3]
            for kc in range(NCH):
                p.op("tensor", (lambda h, kc=kc, slot=slot: h.ldweights(
                    slot[:, kc * 128:(kc + 1) * 128])),
                    waits=[(s_wl, wl_m), (s_ve, ve_xsb)] +
                          (ps.wait_of(sp) if kc == 0 else []), inc=s_te)
                nb0 = 0
                while nb0 < nn:
                    w_ = min(512, nn - nb0)
                    p.op("tensor", (lambda h, kc=kc, nb0=nb0, w_=w_, sp=sp,
                                    rc0=rc0, slot=slot: h.matmul(
                        PS1[:, sp[nb0 // 512] * 512: sp[nb0 // 512] * 512 + w_],
                        slot[:, kc * 128:(kc + 1) * 128],
                        xsb[:, kc * 768 + rc0 + nb0: kc * 768 + rc0 + nb0 + w_],
                        start=(kc == 0), stop=(kc == NCH - 1))), inc=s_te)
                    nb0 += w_
            st8["slab2"][m % 4] = (s_te, s_te.n)
            te_m = s_te.n
            if is_q:
                p.op("vector", (lambda h, m=m, sp=sp: h.tensor_copy(
                    qtil[:, m * 768 + 256: m * 768 + 768], PS(sp[0]))),
                    waits=[(s_te, te_m), (s_te, st8["qtil_te"]),
                           (s_se, st8["qk_se"])], inc=s_ve)
                ps.consumed([sp[0]], s_ve, s_ve.n)
            else:
                c = m - 8
                p.op("vector", (lambda h, c=c, sp=sp, xcol0=xcol0: h.tensor_copy(
                    ktil[:, c * 768 + xcol0: c * 768 + xcol0 + 512], PS(sp[0]))),
                    waits=[(s_te, te_m), (s_te, st8["ktil_te"]),
                           (s_se, st8["qk_se"])], inc=s_ve)
                if span == 768:
                    p.op("vector", (lambda h, c=c, sp=sp: h.tensor_copy(
                        ktil[:, c * 768 + 512: c * 768 + 768],
                        PS(sp[1], width=256))), inc=s_ve)
                ps.consumed(sp, s_ve, s_ve.n)
        if Q == 0:
            p.op("gpsimd", (lambda h, li=li: h.dma_start(
                out=wvp.rearrange("p (c n) -> p c n", c=NCH),
                in_=wqkvI.ap()[li, :, 2 * D:3 * D]
                    .rearrange("(c p) n -> p c n", p=128))),
                waits=[st8["wvp"]], inc=s_wl, inc_by=16)
        wl_v = s_wl.n
        ntile = 6 if first else 4
        for tt in range(ntile):
            vcol = (tt if first else tt + 2) * D
            xo = xcol0 + tt * 128
            sp = [0, 1] if tt % 2 == 0 else [2, 3]
            for kc in range(NCH):
                p.op("tensor", (lambda h, kc=kc, xo=xo: h.ldweights(
                    xsb[:, kc * 768 + xo: kc * 768 + xo + 128])),
                    waits=[(s_wl, wl_v), (s_ve, ve_xsb)] +
                          (ps.wait_of(sp) if kc == 0 else []), inc=s_te)
                for nb in range(2):
                    p.op("tensor", (lambda h, kc=kc, nb=nb, sp=sp, xo=xo: h.matmul(
                        PS(sp[nb]), xsb[:, kc * 768 + xo: kc * 768 + xo + 128],
                        wvp[:, kc * D + nb * 512: kc * D + (nb + 1) * 512],
                        start=(kc == 0), stop=(kc == NCH - 1))), inc=s_te)
            te_t = s_te.n
            p.op("vector", (lambda h, sp=sp, vcol=vcol: h.tensor_copy(
                vtm[:, vcol:vcol + D],
                PS1[:, sp[0] * 512: sp[0] * 512 + D])),
                waits=[(s_te, te_t), (s_te, st8["vtm_te"])], inc=s_ve)
            ps.consumed(sp, s_ve, s_ve.n)
        st8["wvp"] = (s_te, s_te.n)

        norm_rot(qtil, 256, 512, qsb[:, li:li + 1],
                 tabb[:, 0:768], tabb[:, 768:1536], 0, wl_tab)
        ve_rot = norm_rot(ktil, xcol0, span, ksb[:, li:li + 1],
                          tabb[:, 1536:2304], tabb[:, 2304:3072], 768, wl_tab)
        st8["tab_ve"] = ve_rot
        if isinstance(DEBUG_STAGE, tuple) and DEBUG_STAGE[0] == "dump" and li == 0 and Q == 0:
            what = DEBUG_STAGE[1]
            buf = {"qtil": qtil, "ktil": ktil, "vtm": vtm, "rnb": rnb,
                   "xsb": xsb, "tabb": tabb}[what]
            dump_arena_and_stop(buf)
            raise _DumpDone()

        for qb in range(4):
            qbg = Q * 4 + qb
            for hg in range(4):
                pp_slot = (qb * 4 + hg) % 2
                asb = attn_sb[pp_slot]
                atb = attnT_sb[pp_slot]
                for hh in range(4):
                    hd = hg * 4 + hh
                    c, po = hd // 2, (hd % 2) * 64
                    p.op("tensor", (lambda h, c=c, po=po, qb=qb, hh=hh: h.matmul(
                        PS1[:, hh * 512: hh * 512 + 384],
                        qtil[:, c * 768 + 256 + qb * 128: c * 768 + 384 + qb * 128][po:po + 64, :],
                        ktil[:, c * 768 + qb * 128: c * 768 + qb * 128 + 384][po:po + 64, :],
                        start=True, stop=True)),
                        waits=[(s_ve, ve_rot)] + ps.wait_of([hh]), inc=s_te)
                te_sim = s_te.n
                sim4 = PS1.ap().rearrange("p (g n) -> p g n", g=4)
                madds = []
                if qbg == 0:
                    madds = [(0, 2), (1, 3), (2, 1)]
                elif qbg == 1:
                    madds = [(0, 4), (2, 1)]
                else:
                    madds = [(0, 0), (2, 1)]
                for (rc, mi) in madds:
                    p.op("vector", (lambda h, rc=rc, mi=mi, sim4=sim4: h.tensor_tensor(
                        sim4[:, :, rc * 128:(rc + 1) * 128],
                        sim4[:, :, rc * 128:(rc + 1) * 128],
                        maskb[:, mi * 128:(mi + 1) * 128].unsqueeze(1)
                            .broadcast_to([128, 4, 128]), ALU.add)),
                        waits=[(s_te, te_sim)], inc=s_ve)
                p.op("vector", (lambda h, hg=hg, sim4=sim4: h.tensor_reduce(
                    nmx[:, hg * 4:(hg + 1) * 4], sim4[:, :, 0:384],
                    axis=AX.X, op=ALU.max, negate=True)),
                    waits=[st8["nmx_free"][hg]], inc=s_ve)
                ve_mask = s_ve.n
                if (isinstance(DEBUG_STAGE, tuple) and DEBUG_STAGE[0] == "dump3"
                        and li == 0 and Q == 0 and qb == 0 and hg == 0):
                    for hh2 in range(4):
                        p.op("vector", (lambda h, hh2=hh2: h.tensor_copy(
                            xsb[:, 0:3072].bitcast(F32)[:, hh2 * 384:(hh2 + 1) * 384],
                            PS1[:, hh2 * 512: hh2 * 512 + 384])),
                            waits=[(s_ve, ve_mask)], inc=s_ve)
                    dump_arena_and_stop(xsb[:, 0:3072].bitcast(F32))
                    raise _DumpDone()
                scr0 = q2s[:, 0:768]
                scr1 = us[:, 0:768]
                for hh in range(4):
                    hd = hg * 4 + hh
                    sdst = (scr0 if hh < 2 else scr1)[:, (hh % 2) * 384:(hh % 2) * 384 + 384]
                    p.op("scalar", (lambda h, hd=hd, hh=hh, sdst=sdst: h.activation(
                        sdst, PS1[:, hh * 512: hh * 512 + 384], AF.Exp,
                        bias=nmx[:, hd:hd + 1], scale=1.0,
                        accum_out=den[:, hd:hd + 1])),
                        waits=[(s_ve, ve_mask), (s_ve, scr_free[0]),
                               (s_te, q2s_free[0]),
                               st8["den_free"][hg]], inc=s_se)
                se_exp = s_se.n
                if (isinstance(DEBUG_STAGE, tuple) and DEBUG_STAGE[0] == "dump8"
                        and li == 0 and Q == 0 and qb == 0 and hg == 0):
                    dump_arena_and_stop(q2s[:, 0:768])
                    raise _DumpDone()
                st8["nmx_free"][hg] = (s_se, se_exp)
                if (isinstance(DEBUG_STAGE, tuple) and DEBUG_STAGE[0] == "dump7"
                        and li == 0 and Q == 0 and qb == 0 and hg == 0):
                    scr = xsb[:, 0:128].bitcast(F32)
                    p.op("vector", (lambda h, scr=scr: h.tensor_copy(
                        scr[:, 0:16], nmx[:, :])),
                        waits=[(s_se, se_exp)], inc=s_ve)
                    p.op("vector", (lambda h, scr=scr: h.tensor_copy(
                        scr[:, 16:32], den[:, :])), inc=s_ve)
                    dump_arena_and_stop(xsb[:, 0:128].bitcast(F32))
                    raise _DumpDone()
                ps.consumed([0, 1, 2, 3], s_se, se_exp)
                p.op("vector", (lambda h, hg=hg: h.reciprocal(
                    rden[:, hg * 4:(hg + 1) * 4], den[:, hg * 4:(hg + 1) * 4])),
                    waits=[(s_se, se_exp)], inc=s_ve)
                st8["den_free"][hg] = (s_ve, s_ve.n)
                p.op("vector", (lambda h, hg=hg: h.tensor_copy(
                    vpad[:, 0:4], rden[:, hg * 4:(hg + 1) * 4])), inc=s_ve)
                for hh in range(4):
                    hd = hg * 4 + hh
                    ssrc = (scr0 if hh < 2 else scr1)[:, (hh % 2) * 384:(hh % 2) * 384 + 384]
                    p.op("vector", (lambda h, hh=hh, hd=hd, asb=asb, ssrc=ssrc: h.tensor_scalar(
                        out=asb[:, hh * 384:(hh + 1) * 384],
                        in0=ssrc,
                        scalar1=rden[:, hd:hd + 1], scalar2=None,
                        op0=ALU.mult)),
                        waits=[st8["attn_free"][pp_slot]], inc=s_ve)
                ve_scaled = s_ve.n
                scr_free[0] = s_ve.n
                if (isinstance(DEBUG_STAGE, tuple) and DEBUG_STAGE[0] == "dump5"
                        and li == 0 and Q == 0 and qb == 0 and hg == 0):
                    dump_arena_and_stop(asb)
                    raise _DumpDone()
                for hh in range(4):
                    for jc in range(3):
                        p.op("tensor", (lambda h, hh=hh, jc=jc, asb=asb: h.transpose(
                            PT[:, hh * 1024 + jc * 128: hh * 1024 + (jc + 1) * 128],
                            asb[:, hh * 384 + jc * 128: hh * 384 + (jc + 1) * 128],
                            eyeb[:, :])),
                            waits=[(s_ve, ve_scaled)] +
                                  (ps.wait_of([4, 5, 6, 7]) if (hh == 0 and jc == 0) else []),
                            inc=s_te)
                te_tr = s_te.n
                st8["attn_free"][pp_slot] = (s_te, te_tr)
                for hh in range(4):
                    p.op("vector", (lambda h, hh=hh, atb=atb: h.tensor_copy(
                        atb[:, hh * 384:(hh + 1) * 384],
                        PT[:, hh * 1024: hh * 1024 + 384])),
                        waits=[(s_te, te_tr)] +
                              ([st8["attnT_free"][pp_slot]] if hh == 0 else []),
                        inc=s_ve)
                ve_at = s_ve.n
                ps.consumed([4, 5, 6, 7], s_ve, ve_at)
                if (isinstance(DEBUG_STAGE, tuple) and DEBUG_STAGE[0] == "dump6"
                        and li == 0 and Q == 0 and qb == 0 and hg == 0):
                    dump_arena_and_stop(atb)
                    raise _DumpDone()
                for pr in range(2):
                    hd0 = hg * 4 + 2 * pr
                    pv_slot = 6 + pr
                    for par in range(2):
                        hd = hd0 + par
                        for jc in range(3):
                            vt = (qb + jc) * D
                            p.op("tensor", (lambda h, vt=vt, hd=hd: h.ldweights(
                                vtm[:, vt + hd * 64: vt + (hd + 1) * 64])),
                                waits=[(s_ve, ve_at)] +
                                      (ps.wait_of([pv_slot]) if (jc == 0 and par == 0) else []),
                                inc=s_te)
                            p.op("tensor", (lambda h, vt=vt, hd=hd, jc=jc, par=par,
                                            pv_slot=pv_slot, atb=atb, pr=pr: h.matmul(
                                PS(pv_slot, width=128)[0:64, :] if par == 0 else
                                PS2[64:128, (pv_slot % 4) * 512:(pv_slot % 4) * 512 + 128],
                                vtm[:, vt + hd * 64: vt + (hd + 1) * 64],
                                atb[:, (2 * pr + par) * 384 + jc * 128:
                                    (2 * pr + par) * 384 + (jc + 1) * 128],
                                start=(jc == 0), stop=(jc == 2),
                                tile_position=(0, 64) if par == 1 else None)),
                                inc=s_te)
                    te_pv = s_te.n
                    ic = hd0 // 2
                    p.op("vector", (lambda h, ic=ic, qb=qb, pv_slot=pv_slot: h.tensor_copy(
                        aout[:, ic * 512 + qb * 128: ic * 512 + (qb + 1) * 128],
                        PS(pv_slot, width=128))),
                        waits=[(s_te, te_pv), (s_te, st8["aout_te"])], inc=s_ve)
                    ps.consumed([pv_slot], s_ve, s_ve.n)
                st8["attnT_free"][pp_slot] = (s_te, s_te.n)
        ve_attn = s_ve.n
        if isinstance(DEBUG_STAGE, tuple) and DEBUG_STAGE[0] == "dump2" and li == 0 and Q == 0:
            what = DEBUG_STAGE[1]
            buf = {"aout": aout, "attn0": attn_sb[0], "attn1": attn_sb[1],
                   "attnT0": attnT_sb[0], "attnT1": attnT_sb[1]}[what]
            dump_arena_and_stop(buf)
            raise _DumpDone()

        for fc in range(NCH):
            slot = wsl2[fc % 4]
            p.op("gpsimd", (lambda h, fc=fc, li=li, slot=slot: h.dma_start(
                out=slot.rearrange("p (c n) -> p c n", c=NCH),
                in_=woI.ap()[li, :, fc * 128:(fc + 1) * 128]
                    .rearrange("(c p) n -> p c n", p=128))),
                waits=[st8["slab2"][fc % 4]], inc=s_wl, inc_by=16)
            wl_f = s_wl.n
            sp = [fc % 2]
            for ic in range(NCH):
                p.op("tensor", (lambda h, ic=ic, slot=slot: h.ldweights(
                    slot[:, ic * 128:(ic + 1) * 128])),
                    waits=[(s_wl, wl_f), (s_ve, ve_attn)] +
                          (ps.wait_of(sp) if ic == 0 else []), inc=s_te)
                p.op("tensor", (lambda h, ic=ic, sp=sp, slot=slot: h.matmul(
                    PS(sp[0]), slot[:, ic * 128:(ic + 1) * 128],
                    aout[:, ic * 512:(ic + 1) * 512],
                    start=(ic == 0), stop=(ic == NCH - 1))), inc=s_te)
            st8["slab2"][fc % 4] = (s_te, s_te.n)
            te_f = s_te.n
            p.op("vector", (lambda h, fc=fc, sp=sp, Qs=Qs: h.tensor_tensor(
                stream_ap(fc, Qs, Qs + QT - 1), stream_ap(fc, Qs, Qs + QT - 1),
                PS(sp[0])[:, 0:QT - 1], ALU.add)), waits=[(s_te, te_f)], inc=s_ve)
            p.op("vector", (lambda h, fc=fc, sp=sp: h.tensor_copy(
                dcol[:, fc:fc + 1], PS(sp[0])[:, QT - 1:QT])), inc=s_ve)
            ps.consumed(sp, s_ve, s_ve.n)
        st8["aout_te"] = s_te.n
        st8["xsb_te"] = s_te.n
        st8["qtil_te"] = s_te.n
        st8["ktil_te"] = s_te.n
        st8["vtm_te"] = s_te.n
        st8["qk_se"] = s_se.n

        if Q < 3:
            p.op("vector", (lambda h: h.tensor_copy(
                ktil.rearrange("p (c n) -> p c n", c=NCH)[:, :, 0:256],
                ktil.rearrange("p (c n) -> p c n", c=NCH)[:, :, 512:768])),
                waits=[(s_te, s_te.n)], inc=s_ve)
            p.op("vector", (lambda h: h.tensor_copy(
                vtm[:, 0:2 * D], vtm[:, 4 * D:6 * D])), inc=s_ve)

    def attn_sublayer(li):
        for Q in range(4):
            attn_quarter(li, Q)
        for c in range(NCH):
            p.op("vector", (lambda h, c=c: h.tensor_tensor(
                stream_ap(c, T - 1, T), stream_ap(c, T - 1, T),
                dcol[:, c:c + 1], ALU.add)), inc=s_ve)
        st8["stream_ve"] = s_ve.n

    def ffn_sublayer(li):
        p.op("gpsimd", (lambda h, li=li: h.dma_start(
            out=w2res.rearrange("p (c n) -> p c n", c=NM1)[:, 0:21, :],
            in_=w2I.ap()[li, 0:21 * 128, :].rearrange("(c p) n -> p c n", p=128))),
            waits=[st8["w2res"]], inc=s_wl, inc_by=16)
        p.op("gpsimd", (lambda h, li=li: h.dma_start(
            out=w2res[0:LASTW, 21 * 1024:22 * 1024],
            in_=w2I.ap()[li, 21 * 128:DI, :])), inc=s_wl, inc_by=16)
        wl_w2 = s_wl.n
        for qf in range(4):
            t0 = qf * FQ
            wl_bc = ln_to_bcast(t0, True, xslF, x2slF,
                                ready=[(s_ve, st8["stream_ve"])], span=FQ)
            for c in range(NCH):
                p.op("vector", (lambda h, c=c, t0=t0: h.tensor_tensor(
                    t32F[:, 0:FQ], xs_ap(c, t0, t0 + FQ), bc1[:, 0:FQ],
                    ALU.mult)), waits=[(s_wl, wl_bc)], inc=s_ve)
                p.op("vector", (lambda h: h.tensor_tensor(
                    t32F[:, 0:FQ], t32F[:, 0:FQ], bc2[:, 0:FQ], ALU.add)),
                    inc=s_ve)
                p.op("vector", (lambda h, c=c, li=li: h.tensor_scalar(
                    out=xnf[:, c * FQ:(c + 1) * FQ], in0=t32F[:, 0:FQ],
                    scalar1=lngb[:, li * NCH + c: li * NCH + c + 1],
                    scalar2=lnbb[:, li * NCH + c: li * NCH + c + 1],
                    op0=ALU.mult, op1=ALU.add)),
                    waits=[(s_te, xnf_te[0])], inc=s_ve)
            if qf > 0:
                for c in range(NCH):
                    p.op("vector", (lambda h, c=c, t0=t0: h.tensor_tensor(
                        stream_ap(c, t0 - 1, t0), stream_ap(c, t0 - 1, t0),
                        dcol[:, c:c + 1], ALU.add)), inc=s_ve)
            ve_xn = s_ve.n
            for m in range(NM1):
                mw = 128 if m < 21 else LASTW
                slot = w1sl[m % 2]
                p.op("gpsimd", (lambda h, m=m, li=li, mw=mw, slot=slot: h.dma_start(
                    out=slot.rearrange("p (c n) -> p c n", c=NCH)[:, :, 0:mw],
                    in_=w1I.ap()[li, :, m * 128:m * 128 + mw]
                        .rearrange("(c p) n -> p c n", p=128))),
                    waits=[st8["w1"][m % 2]], inc=s_wl, inc_by=16)
                p.op("gpsimd", (lambda h, m=m, li=li, mw=mw, slot=slot: h.dma_start(
                    out=slot.rearrange("p (c n) -> p c n", c=NCH)[:, :, 128:128 + mw],
                    in_=w1I.ap()[li, :, DI + m * 128:DI + m * 128 + mw]
                        .rearrange("(c p) n -> p c n", p=128))),
                    inc=s_wl, inc_by=16)
                wl_m = s_wl.n
                spa = [0] if m % 2 == 0 else [2]
                spg = [1] if m % 2 == 0 else [3]
                for kc in range(NCH):
                    p.op("tensor", (lambda h, kc=kc, mw=mw, slot=slot: h.ldweights(
                        slot[:, kc * 256: kc * 256 + mw])),
                        waits=[(s_wl, wl_m), (s_ve, ve_xn)] +
                              (ps.wait_of(spa) if kc == 0 else []), inc=s_te)
                    p.op("tensor", (lambda h, kc=kc, spa=spa, mw=mw, slot=slot: h.matmul(
                        PS(spa[0])[0:mw, :], slot[:, kc * 256: kc * 256 + mw],
                        xnf[:, kc * FQ:(kc + 1) * FQ],
                        start=(kc == 0), stop=(kc == NCH - 1))), inc=s_te)
                    p.op("tensor", (lambda h, kc=kc, mw=mw, slot=slot: h.ldweights(
                        slot[:, kc * 256 + 128: kc * 256 + 128 + mw])),
                        waits=(ps.wait_of(spg) if kc == 0 else []), inc=s_te)
                    p.op("tensor", (lambda h, kc=kc, spg=spg, mw=mw, slot=slot: h.matmul(
                        PS(spg[0])[0:mw, :],
                        slot[:, kc * 256 + 128: kc * 256 + 128 + mw],
                        xnf[:, kc * FQ:(kc + 1) * FQ],
                        start=(kc == 0), stop=(kc == NCH - 1))), inc=s_te)
                st8["w1"][m % 2] = (s_te, s_te.n)
                te_m = s_te.n
                p.op("scalar", (lambda h, m=m, li=li, mw=mw, spg=spg: h.activation(
                    ggs[0:mw, 0:FQ], PS(spg[0])[0:mw, :],
                    AF.Gelu, bias=b1gb[0:mw, li * NM1 + m: li * NM1 + m + 1],
                    scale=1.0)),
                    waits=[(s_te, te_m), (s_ve, ggs_free[0])], inc=s_se)
                se_g = s_se.n
                ps.consumed(spg, s_se, se_g)
                p.op("vector", (lambda h, m=m, li=li, mw=mw, spa=spa: h.scalar_tensor_tensor(
                    out=hbuf[0:mw, m * FQ:(m + 1) * FQ],
                    in0=PS(spa[0])[0:mw, :],
                    scalar=b1ab[0:mw, li * NM1 + m: li * NM1 + m + 1],
                    in1=ggs[0:mw, 0:FQ], op0=ALU.add, op1=ALU.mult)),
                    waits=[(s_se, se_g), (s_te, hbuf_te[0])], inc=s_ve)
                ps.consumed(spa, s_ve, s_ve.n)
                ggs_free[0] = s_ve.n
            ve_h = s_ve.n
            for fc in range(NCH):
                sp = [4 + fc % 2]
                for kc in range(NM1):
                    mw = 128 if kc < 21 else LASTW
                    p.op("tensor", (lambda h, kc=kc, fc=fc, mw=mw: h.ldweights(
                        w2res[0:mw, kc * 1024 + fc * 128: kc * 1024 + (fc + 1) * 128])),
                        waits=[(s_wl, wl_w2), (s_ve, ve_h)] +
                              (ps.wait_of(sp) if kc == 0 else []), inc=s_te)
                    p.op("tensor", (lambda h, kc=kc, fc=fc, sp=sp, mw=mw: h.matmul(
                        PS(sp[0]),
                        w2res[0:mw, kc * 1024 + fc * 128: kc * 1024 + (fc + 1) * 128],
                        hbuf[0:mw, kc * FQ:(kc + 1) * FQ],
                        start=(kc == 0), stop=(kc == NM1 - 1))), inc=s_te)
                te_f = s_te.n
                p.op("vector", (lambda h, fc=fc, sp=sp, li=li, t0=t0: h.scalar_tensor_tensor(
                    out=stream_ap(fc, t0, t0 + FQ - 1), in0=PS(sp[0])[:, 0:FQ - 1],
                    scalar=b2b[:, li * NCH + fc: li * NCH + fc + 1],
                    in1=stream_ap(fc, t0, t0 + FQ - 1),
                    op0=ALU.add, op1=ALU.add)),
                    waits=[(s_te, te_f)], inc=s_ve)
                p.op("vector", (lambda h, fc=fc, sp=sp, li=li: h.tensor_scalar(
                    out=dcol[:, fc:fc + 1], in0=PS(sp[0])[:, FQ - 1:FQ],
                    scalar1=b2b[:, li * NCH + fc: li * NCH + fc + 1],
                    scalar2=None, op0=ALU.add)), inc=s_ve)
                ps.consumed(sp, s_ve, s_ve.n)
            hbuf_te[0] = s_te.n
            xnf_te[0] = s_te.n
        for c in range(NCH):
            p.op("vector", (lambda h, c=c: h.tensor_tensor(
                stream_ap(c, T - 1, T), stream_ap(c, T - 1, T),
                dcol[:, c:c + 1], ALU.add)), inc=s_ve)
        st8["w2res"] = (s_te, s_te.n)
        st8["stream_ve"] = s_ve.n

    ggs_free = [0]
    hbuf_te = [0]
    xnf_te = [0]

    class _DumpDone(Exception):
        pass

    stop_at = DEBUG_STAGE[1] if (isinstance(DEBUG_STAGE, tuple) and
                                 DEBUG_STAGE[0] == "mini") else None

    def _run_layers():
        n_sub = 0
        for li in range(LAYERS):
            attn_sublayer(li)
            n_sub += 1
            if stop_at == n_sub:
                return True
            halo_ag()
            st8["stream_ve"] = s_ve.n
            barrier()
            ffn_sublayer(li)
            n_sub += 1
            if stop_at == n_sub:
                return True
            if n_sub < 2 * LAYERS:
                halo_ag()
                st8["stream_ve"] = s_ve.n
            barrier()
        return False

    try:
        done = _run_layers()
    except _DumpDone:
        return nc, p
    if done:
        dump_stream_and_stop()
        return nc, p

    x1slot = arena[:, 0:2 * T].bitcast(F32)
    for c in range(NCH):
        p.op("sync", (lambda h, c=c: h.dma_start(
            out=x1slot[:, 0:T],
            in_=x1sp[c * 128:(c + 1) * 128, :])),
            waits=[(s_ve, s_ve.n)], inc=s_sy, inc_by=16)
        sy_c = s_sy.n
        p.op("vector", (lambda h, c=c: h.tensor_tensor(
            stream_ap(c, 0, T), stream_ap(c, 0, T), x1slot[:, 0:T], ALU.add)),
            waits=[(s_sy, sy_c)], inc=s_ve)
        ve_c = s_ve.n
        p.op("sync", (lambda h, c=c: h.dma_start(
            out=outE[c * 128:(c + 1) * 128, :], in_=stream_ap(c, 0, T))),
            waits=[(s_ve, ve_c)], inc=s_st, inc_by=16)
        p.wait("sync", [(s_st, s_st.n)])
    p.wait("sync", [(s_st, s_st.n)])
    return nc, p


def emit(nc, p):
    with nc.Block() as block:
        @block.tensor
        def _(h):
            for f in p.ops["tensor"]:
                f(h)

        @block.vector
        def _(h):
            for f in p.ops["vector"]:
                f(h)

        @block.scalar
        def _(h):
            for f in p.ops["scalar"]:
                f(h)

        @block.gpsimd
        def _(h):
            for f in p.ops["gpsimd"]:
                f(h)

        @block.sync
        def _(h):
            for f in p.ops["sync"]:
                f(h)
    return nc


def kernel(L0_attn_x, L0_attn_W_qkv, L0_attn_W_out, L0_attn_b_out,
           L0_ffn_Wqkv, L0_ffn_q_scale, L0_ffn_k_scale, L0_ffn_Wo,
           L0_ffn_ln_g, L0_ffn_ln_b, L0_ffn_W1, L0_ffn_b1, L0_ffn_W2,
           L0_ffn_b2):
    f32 = np.float32
    x = np.asarray(L0_attn_x, f32)
    wqkvoT = np.ascontiguousarray(np.asarray(L0_attn_W_qkv, f32).T)
    woutT = np.ascontiguousarray(np.asarray(L0_attn_W_out, f32).T)
    bout = np.ascontiguousarray(np.asarray(L0_attn_b_out, f32).reshape(NCH, 128).T)
    wqkv = _permute_qk_cols(np.asarray(L0_ffn_Wqkv, f32))
    wo = np.ascontiguousarray(np.asarray(L0_ffn_Wo, f32))
    w1 = np.ascontiguousarray(np.asarray(L0_ffn_W1, f32))
    w2 = np.ascontiguousarray(np.asarray(L0_ffn_W2, f32))
    qs = np.asarray(L0_ffn_q_scale, f32)
    ks = np.asarray(L0_ffn_k_scale, f32)
    qsv = np.stack([np.concatenate([qs[i][_PERM64], qs[i][_PERM64]])
                    for i in range(LAYERS)], axis=1)
    ksv = np.stack([np.concatenate([ks[i][_PERM64], ks[i][_PERM64]])
                    for i in range(LAYERS)], axis=1)
    lng = np.asarray(L0_ffn_ln_g, f32)
    lnb = np.asarray(L0_ffn_ln_b, f32)
    lngc = np.zeros((128, NCH * LAYERS), f32)
    lnbc = np.zeros((128, NCH * LAYERS), f32)
    for i in range(LAYERS):
        for c in range(NCH):
            lngc[:, i * NCH + c] = lng[i, c * 128:(c + 1) * 128]
            lnbc[:, i * NCH + c] = lnb[i, c * 128:(c + 1) * 128]
    b1 = np.asarray(L0_ffn_b1, f32)
    b1ac = np.zeros((128, NM1 * LAYERS), f32)
    b1gc = np.zeros((128, NM1 * LAYERS), f32)
    for i in range(LAYERS):
        for m in range(NM1):
            mw = 128 if m < 21 else LASTW
            b1ac[0:mw, i * NM1 + m] = b1[i, m * 128:m * 128 + mw]
            b1gc[0:mw, i * NM1 + m] = b1[i, DI + m * 128:DI + m * 128 + mw]
    b2 = np.asarray(L0_ffn_b2, f32)
    b2c = np.zeros((128, NCH * LAYERS), f32)
    for i in range(LAYERS):
        for c in range(NCH):
            b2c[:, i * NCH + c] = b2[i, c * 128:(c + 1) * 128]
    ones2 = np.zeros((128, NCH * 16), f32)
    for c in range(NCH):
        ones2[0:64, c * 16 + 2 * c] = 1.0
        ones2[64:128, c * 16 + 2 * c + 1] = 1.0
    eye = np.eye(128, dtype=f32)

    nc, p = build_program()
    emit(nc, p)

    in_maps = []
    for core in range(8):
        b, hh = core // 2, core % 2
        xT = np.ascontiguousarray(x[b].T[:, hh * T:(hh + 1) * T])
        in_maps.append({
            "xT": xT, "wqkvoT": wqkvoT, "woutT": woutT, "boutI": bout,
            "wqkvI": wqkv, "woI": wo, "w1I": w1, "w2I": w2,
            "qsI": qsv, "ksI": ksv, "lngI": lngc, "lnbI": lnbc,
            "b1aI": b1ac, "b1gI": b1gc, "b2I": b2c,
            "tabsI": _rotary_tables(hh * T),
            "masksI": _masks_host(hh),
            "ones2I": ones2, "eyeI": eye,
            "hvI": np.full((128, 1), float(hh), f32),
        })
    res = run_bass_kernel_spmd(nc, in_maps, core_ids=list(range(8)))
    out = np.zeros((B, N, D), f32)
    for core in range(8):
        b, hh = core // 2, core % 2
        out[b, hh * T:(hh + 1) * T, :] = res.results[core]["outE"].T
    return out
